# version 08
# baseline (speedup 1.0000x reference)
import math
import numpy as np

import concourse.bass as bass
import concourse.mybir as mybir
from concourse.bass_utils import run_bass_kernel_spmd
from concourse.alu_op_type import AluOpType as AO

F32 = mybir.dt.float32
BF16 = mybir.dt.bfloat16
U8 = mybir.dt.uint8
AF = mybir.ActivationFunctionType
AX = mybir.AxisListType

D = 1024
S = 4096
C = 8
TT = 512
NT = S // TT
WF = 4096
WC = 32
NQ = 32
EPS = 1e-5
T1 = math.log(0.95 / 0.05)
T2 = math.log(0.85 / 0.15)
T3 = math.log(0.25 / 0.75)
T4 = math.log(0.10 / 0.90)
K27 = 2.0 / 7.0

SCRATCH_BYTES = 75776
import os
QPP = os.environ.get("K_QPP", "1") == "1"
A1IL = os.environ.get("K_A1IL", "1") == "1"



def build_nc(stage=4):
    nc = bass.Bass("TRN2", target_bir_lowering=False, detect_race_conditions=False)

    x = nc.dram_tensor("x", [D, S], F32, kind="ExternalInput").ap()
    wqkv = nc.dram_tensor("wqkv", [D, 3 * D], F32, kind="ExternalInput").ap()
    bqkv = nc.dram_tensor("bqkv", [3 * D], F32, kind="ExternalInput").ap()
    wproj = nc.dram_tensor("wproj", [D, D], F32, kind="ExternalInput").ap()
    bproj = nc.dram_tensor("bproj", [D], F32, kind="ExternalInput").ap()
    w1 = nc.dram_tensor("w1", [D, WF], F32, kind="ExternalInput").ap()
    b1 = nc.dram_tensor("b1", [WF], F32, kind="ExternalInput").ap()
    w2 = nc.dram_tensor("w2", [WF, D], F32, kind="ExternalInput").ap()
    b2 = nc.dram_tensor("b2", [D], F32, kind="ExternalInput").ap()
    y = nc.dram_tensor("y", [D, S], F32, kind="ExternalOutput").ap()
    dbg = os.environ.get("K_DBG", "0") == "1"
    ik = dict(kind="ExternalOutput") if dbg else {}
    x1h = nc.dram_tensor("x1h", [D, S], F32, **ik).ap()
    dlh = nc.dram_tensor("dlh", [D, S], F32, **ik).ap()
    xlh = nc.dram_tensor("xlh", [D, S], BF16, **ik).ap()
    qh = nc.dram_tensor("qh", [NQ, 128, C, 128], BF16, **ik).ap()

    scratch = nc.alloc_sbuf_tensor("scratch", [128, SCRATCH_BYTES], U8)
    base = nc.lookup_mloc(scratch).addr

    def carve(name, shape, dtype, off):
        h = nc.virtual_sbuf_tensor(name, shape, dtype)
        nc._tensor(h.name, h.shape, h.dtype, type="SB")
        mloc = nc.lookup_mloc(h)
        mloc.allocated = True
        mloc.addr = base + off
        return h

    xa = carve("xa", [128, C * TT], F32, 0)
    xa2 = carve("xa2", [128, C * TT], F32, 16384)
    B1 = carve("B1", [128, C * TT], BF16, 32768)
    lnv0 = carve("lnv0", [128, TT], F32, 32768)
    lnv1 = carve("lnv1", [128, TT], F32, 34816)
    t1s = carve("t1s", [128, TT], F32, 40960)
    h_sb = carve("h_sb", [128, WC * TT], BF16, 43008)
    wk_a1 = carve("wk_a1", [128, C * D], BF16, 16384)
    wv_a1 = carve("wv_a1", [128, C * D], BF16, 43008)
    qstage = carve("qstage", [128, C * TT], BF16, 0)
    s_sb = carve("s_sb", [128, S], F32, 0)
    work = carve("work", [128, S], F32, 16384)
    p_sb = carve("p_sb", [128, S], BF16, 32768)
    qg0 = carve("qg0", [128, C * 128], BF16, 40960)
    qg1 = carve("qg1", [128, C * 128], BF16, 43008)
    delta = carve("delta", [128, C * 128], F32, 45056)
    aout = carve("aout", [128, D], BF16, 49152)
    wqp = carve("wqp", [128, C * D], BF16, 59392)

    bigA = nc.alloc_sbuf_tensor("bigA", [128, C * S], BF16)
    bigB = nc.alloc_sbuf_tensor("bigB", [128, WC * D], BF16)
    ident = nc.alloc_sbuf_tensor("ident", [128, 128], BF16)
    inv_bc = nc.alloc_sbuf_tensor("inv_bc", [128, 128], BF16)
    ones_row = nc.alloc_sbuf_tensor("ones_row", [1, TT], BF16)
    bvrow = nc.alloc_sbuf_tensor("bvrow", [1, D], BF16)
    bias_cols = nc.alloc_sbuf_tensor("bias_cols", [128, 64], F32)
    smalls = nc.alloc_sbuf_tensor("smalls", [128, 8], F32)

    pb = [nc.alloc_psum_tensor(f"pb{i}", [128, 512], F32) for i in range(6)]
    pt = [nc.alloc_psum_tensor(f"pt{i}", [128, 1024], BF16) for i in range(2)]

    ops = {k: [] for k in "tavgq"}

    class Sem:
        __slots__ = ("h", "v")

        def __init__(self, name):
            self.h = nc.semaphore(name).__enter__()
            self.v = 0

    qS = Sem("qS")
    qG = Sem("qG")
    pe = Sem("pe")
    ac = Sem("ac")
    dv = Sem("dv")
    pl = Sem("pl")
    SEMS = {"t": pe, "a": ac, "v": dv, "g": pl, "q": qS}

    def E(k, fn):
        ops[k].append(fn)

    def EW(k, sem, val):
        assert 0 < val <= sem.v, (k, val, sem.v)
        E(k, lambda e, h=sem.h, v=val: e.wait_ge(h, v))

    def EI(k, fn, sem, n=1):
        sem.v += n
        E(k, lambda e, fn=fn, h=sem.h, n=n: fn(e).then_inc(h, n))
        return sem.v

    guards = {}

    def bwait(k, key):
        if key in guards:
            s, v = guards[key]
            if v > 0:
                EW(k, s, v)

    def bset(key, sem):
        guards[key] = (sem, sem.v)

    def barrier():
        snap = [(qS, qS.v), (qG, qG.v), (pe, pe.v), (ac, ac.v), (dv, dv.v), (pl, pl.v)]
        for k in "tavgq":
            own = SEMS[k]
            for sem, val in snap:
                if val > 0 and sem is not own:
                    EW(k, sem, val)

    def xt3(ap_hbm, t0, tt=TT):
        return ap_hbm.rearrange("(c p) t -> p c t", p=128)[:, :, t0:t0 + tt]

    def sb3(ten, c=C):
        return ten[:, :].rearrange("p (c t) -> p c t", c=c)

    def xsl(buf, c):
        return buf[:, c * TT:(c + 1) * TT]

    E("g", lambda g: g.memset(ident[:, :], 0.0))
    E("g", lambda g: g.affine_select(
        out=ident[:, :], in_=ident[:, :], compare_op=AO.not_equal,
        fill=1.0, base=0, pattern=[[-1, 128]], channel_multiplier=1))
    E("g", lambda g: g.memset(inv_bc[:, :], 1.0 / 1024.0))
    consts_done = EI("g", lambda g: g.memset(ones_row[0:1, :], 1.0), pl)

    st_b1 = h_sb[0:1, 0:WF]
    st_bq = h_sb[0:1, WF:WF + D]
    st_bk = h_sb[0:1, WF + D:WF + 2 * D]
    st_bp = h_sb[0:1, WF + 2 * D:WF + 3 * D]
    st_b2 = h_sb[0:1, WF + 3 * D:WF + 4 * D]
    EI("g", lambda g: g.dma_start(bvrow[0:1, :], bqkv[None, 2 * D:3 * D]), qG, 16)
    EI("g", lambda g: g.dma_start(st_b1, b1[None, :]), qG, 16)
    EI("g", lambda g: g.dma_start(st_bq, bqkv[None, 0:D]), qG, 16)
    EI("g", lambda g: g.dma_start(st_bk, bqkv[None, D:2 * D]), qG, 16)
    EI("g", lambda g: g.dma_start(st_bp, bproj[None, :]), qG, 16)
    qg_bias = EI("g", lambda g: g.dma_start(st_b2, b2[None, :]), qG, 16)
    for c in range(C):
        EI("g", lambda g, c=c: g.dma_start(
            bigA[:, c * WF:(c + 1) * WF], w1[c * 128:(c + 1) * 128, :]), qG, 16)
    qg_w1 = qG.v
    for w in range(WC):
        EI("g", lambda g, w=w: g.dma_start(
            bigB[:, w * D:(w + 1) * D], w2[w * 128:(w + 1) * 128, :]), qG, 16)
    qg_w2 = qG.v

    EW("t", pl, consts_done)
    EW("t", qG, qg_bias)
    bias_srcs = ([(st_b1, w, w) for w in range(WC)]
                 + [(st_bq, e, 32 + e) for e in range(C)]
                 + [(st_bk, e, 40 + e) for e in range(C)]
                 + [(st_bp, e, 48 + e) for e in range(C)]
                 + [(st_b2, e, 56 + e) for e in range(C)])
    for (stg, i, col) in bias_srcs[:-1]:
        E("t", lambda t, stg=stg, i=i, col=col: t.matmul(
            pb[0][:, col:col + 1], stg[0:1, i * 128:(i + 1) * 128],
            ones_row[0:1, 0:1], start=True, stop=True))
    stg_l, i_l, col_l = bias_srcs[-1]
    pe_bias = EI("t", lambda t, stg=stg_l, i=i_l, col=col_l: t.matmul(
        pb[0][:, col:col + 1], stg[0:1, i * 128:(i + 1) * 128],
        ones_row[0:1, 0:1], start=True, stop=True), pe)
    EW("a", pe, pe_bias)
    ac_bias = EI("a", lambda a: a.activation(bias_cols[:, 0:64], pb[0][:, 0:64], AF.Copy), ac)
    bset("pb0", ac)
    EW("v", ac, ac_bias)
    dv_hb2c = EI("v", lambda v: v.tensor_scalar(
        bias_cols[:, 56:64], bias_cols[:, 56:64], 0.5, None, AO.mult), dv)

    qS_load = {}
    EI("q", lambda q: q.dma_start(sb3(xa), xt3(x, 0)), qS, 16)
    qS_load[0] = qS.v
    EI("q", lambda q: q.dma_start(sb3(xa2), xt3(x, TT)), qS, 16)
    qS_load[1] = qS.v

    if stage == 0:
        y3 = y.rearrange("(a p) t -> p a t", p=128)
        EW("q", ac, ac_bias)
        EW("q", dv, dv_hb2c)
        EI("q", lambda q: q.dma_start(y3[:, 0, 0:64], bias_cols[:, :]), qS, 16)
        EW("g", qG, qG.v)
        EI("g", lambda g: g.dma_start(y3[:, 1, :], bigA[:, 0:4096]), qG, 16)
        EI("g", lambda g: g.dma_start(y3[:, 3, 0:128], ident[:, :]), qG, 16)
        EW("q", qS, qS_load[1])
        EI("q", lambda q: q.dma_start(y3[:, 2, :], xa[:, :]), qS, 16)
        EW("q", qS, qS.v)
        EW("g", qG, qG.v)

    def ln_sq(xin, buf, a_pre):
        for (s, v) in a_pre:
            EW("a", s, v)
        return EI("a", lambda a: a.activation(buf[:, :], xin[:, :], AF.Square), ac)

    def ln_ssq(buf, ac_sq):
        bwait("t", "pb0")
        EW("t", ac, ac_sq)
        for c in range(C - 1):
            E("t", lambda t, c=c, buf=buf: t.matmul(
                pb[0][:, :], inv_bc[:, :], buf[:, c * TT:(c + 1) * TT],
                start=(c == 0), stop=False))
        return EI("t", lambda t, buf=buf: t.matmul(
            pb[0][:, :], inv_bc[:, :], buf[:, (C - 1) * TT:C * TT],
            start=False, stop=True), pe)

    def ln_cast(xin, buf, pe_ssq):
        EW("a", pe, pe_ssq)
        return EI("a", lambda a: a.activation(buf[:, :], xin[:, :], AF.Copy), ac)

    def ln_sum(buf, ac_cast):
        bwait("t", "pb1")
        EW("t", ac, ac_cast)
        for c in range(C - 1):
            E("t", lambda t, c=c, buf=buf: t.matmul(
                pb[1][:, :], inv_bc[:, :], buf[:, c * TT:(c + 1) * TT],
                start=(c == 0), stop=False))
        return EI("t", lambda t, buf=buf: t.matmul(
            pb[1][:, :], inv_bc[:, :], buf[:, (C - 1) * TT:C * TT],
            start=False, stop=True), pe)

    def ln_stats2(v0, v1, pe_sum):
        EW("a", pe, pe_sum)
        ac_msq = EI("a", lambda a: a.activation(v0[:, :], pb[1][:, :], AF.Square), ac)
        EW("v", ac, ac_msq)
        dv_var = EI("v", lambda v: v.scalar_tensor_tensor(
            v1[:, :], pb[0][:, :], EPS, v0[:, :], AO.add, AO.subtract), dv)
        EW("a", dv, dv_var)
        E("a", lambda a: a.activation(v0[:, :], v1[:, :], AF.Ln))
        return EI("a", lambda a: a.activation(pb[0][:, :], v0[:, :], AF.Exp, scale=-0.5), ac)

    def ln_xln(xin, buf, ac_rstd):
        EW("v", ac, ac_rstd)
        for c in range(C):
            E("v", lambda v, c=c, xin=xin: v.tensor_tensor(
                t1s[:, :], xin[:, c * TT:(c + 1) * TT], pb[1][:, :], AO.subtract))
            if c < C - 1:
                E("v", lambda v, c=c, buf=buf: v.tensor_tensor(
                    buf[:, c * TT:(c + 1) * TT], t1s[:, :], pb[0][:, :], AO.mult))
        dv_xln = EI("v", lambda v, buf=buf: v.tensor_tensor(
            buf[:, (C - 1) * TT:C * TT], t1s[:, :], pb[0][:, :], AO.mult), dv)
        bset("pb0", dv)
        bset("pb1", dv)
        return dv_xln

    def emit_ln(xin, a_pre, buf=None, v0=None, v1=None):
        buf = B1 if buf is None else buf
        v0 = lnv0 if v0 is None else v0
        v1 = lnv1 if v1 is None else v1
        ac_sq = ln_sq(xin, buf, a_pre)
        pe_ssq = ln_ssq(buf, ac_sq)
        ac_cast = ln_cast(xin, buf, pe_ssq)
        pe_sum = ln_sum(buf, ac_cast)
        ac_rstd = ln_stats2(v0, v1, pe_sum)
        return ln_xln(xin, buf, ac_rstd)

    def emit_ffn(second, w1_gate, w2_gate, acc_gate=None):
        pe_hlast_prev = None
        qS_out_end = {}
        ln_state = {}
        dv_xln_cur = None

        def ffn_halfb2(xin):
            for c in range(C):
                E("v", lambda v, c=c, xin=xin: v.tensor_scalar(
                    xsl(xin, c), xsl(xin, c), bias_cols[:, 56 + c:57 + c], None, AO.add))

        for t in range(NT):
            xin = xa if t % 2 == 0 else xa2
            if t == 0:
                gate = (qS, qS_load[0]) if not second else (qG, acc_gate[0])
                dv_xln_cur = emit_ln(xin, [gate])
                ffn_halfb2(xin)
            EW("t", dv, dv_xln_cur)
            if t == 0:
                EW("t", qG, w1_gate)
            pe_h = {}
            for w in range(WC):
                key = f"pb{2 + w % 2}"
                bwait("t", key)
                for c in range(C - 1):
                    E("t", lambda t_, w=w, c=c: t_.matmul(
                        pb[2 + w % 2][:, :],
                        bigA[:, c * WF + w * 128: c * WF + (w + 1) * 128],
                        B1[:, c * TT:(c + 1) * TT],
                        start=(c == 0), stop=False))
                pe_h[w] = EI("t", lambda t_, w=w: t_.matmul(
                    pb[2 + w % 2][:, :],
                    bigA[:, (C - 1) * WF + w * 128: (C - 1) * WF + (w + 1) * 128],
                    B1[:, (C - 1) * TT:C * TT],
                    start=False, stop=True), pe)
                guards[key] = (pe, pe_h[w])
                EW("a", pe, pe_h[w])
                EI("a", lambda a, w=w: a.activation(
                    h_sb[:, w * TT:(w + 1) * TT], pb[2 + w % 2][:, :], AF.Relu,
                    bias=bias_cols[:, w:w + 1]), ac)
                bset(f"pb{2 + w % 2}", ac)
            ac_hev_last = ac.v
            pe_hlast_prev = pe_h[WC - 1]
            nxt = t + 1 < NT
            xnx = (xa if (t + 1) % 2 == 0 else xa2) if nxt else None
            if nxt:
                gate = (qS, qS_load[t + 1]) if not second else (qG, acc_gate[t + 1])
                ac_sq_n = ln_sq(xnx, B1, [gate, (pe, pe_hlast_prev)])
            EW("t", ac, ac_hev_last)
            if t == 0:
                EW("t", qG, w2_gate)
            for d in range(C):
                key = f"pb{4 + d % 2}"
                bwait("t", key)
                for w in range(WC - 1):
                    E("t", lambda t_, d=d, w=w: t_.matmul(
                        pb[4 + d % 2][:, :],
                        bigB[:, w * D + d * 128: w * D + (d + 1) * 128],
                        h_sb[:, w * TT:(w + 1) * TT],
                        start=(w == 0), stop=False))
                pe_y = EI("t", lambda t_, d=d: t_.matmul(
                    pb[4 + d % 2][:, :],
                    bigB[:, (WC - 1) * D + d * 128: (WC - 1) * D + (d + 1) * 128],
                    h_sb[:, (WC - 1) * TT:WC * TT],
                    start=False, stop=True), pe)
                guards[key] = (pe, pe_y)
                EW("v", pe, pe_y)
                dv_ye = EI("v", lambda v, d=d, xin=xin: v.scalar_tensor_tensor(
                    xsl(xin, d), pb[4 + d % 2][:, :], 0.5, xsl(xin, d),
                    AO.mult, AO.add), dv)
                bset(key, dv)
                dst = (x1h if stage > 1 else y) if not second else y
                EW("q", dv, dv_ye)
                EI("q", lambda q, d=d, t=t, dst=dst, xin=xin: q.dma_start(
                    dst[d * 128:(d + 1) * 128, t * TT:(t + 1) * TT], xsl(xin, d)), qS, 16)
                if nxt and d == 1:
                    pe_ssq_n = ln_ssq(B1, ac_sq_n)
                    ac_cast_n = ln_cast(xnx, B1, pe_ssq_n)
                if nxt and d == 3:
                    pe_sum_n = ln_sum(B1, ac_cast_n)
                    ac_rstd_n = ln_stats2(lnv0, lnv1, pe_sum_n)
                if nxt and d == 4:
                    dv_xln_cur = ln_xln(xnx, B1, ac_rstd_n)
                    ffn_halfb2(xnx)
            qS_out_end[t] = qS.v
            if t + 2 < NT:
                EW("q", qS, qS_out_end[t])
                tgt = xa if (t + 2) % 2 == 0 else xa2
                if not second:
                    EI("q", lambda q, t=t, tgt=tgt: q.dma_start(
                        sb3(tgt), xt3(x, (t + 2) * TT)), qS, 16)
                    qS_load[t + 2] = qS.v
                else:
                    EI("q", lambda q, t=t, tgt=tgt: q.dma_start(
                        sb3(tgt), xt3(x1h, (t + 2) * TT)), qS, 16)
                    qS_load[t + 2] = qS.v
                    EW("g", qS, qS_load[t + 2])
                    EI("g", lambda g, t=t, tgt=tgt: g.dma_start(
                        sb3(tgt), xt3(dlh, (t + 2) * TT), accum_op=AO.add), qG, 16)
                    acc_gate[t + 2] = qG.v
        return qS_out_end[NT - 1]

    snap0 = [(qS, qS.v), (pe, pe.v), (ac, ac.v), (dv, dv.v), (pl, pl.v)]
    for k_ in "tavgq":
        own_ = SEMS[k_]
        for sem_, val_ in snap0:
            if val_ > 0 and sem_ is not own_:
                EW(k_, sem_, val_)
    if stage >= 1:
        qS_x1_done = emit_ffn(False, qg_w1, qg_w2)

    if stage >= 2:
        barrier()

        B1b = carve("B1b", [128, C * TT], BF16, 59392)
        lnv0b = carve("lnv0b", [128, TT], F32, 59392)
        lnv1b = carve("lnv1b", [128, TT], F32, 61440)
        b1q = carve("b1q", [128, C * TT], BF16, 43008)

        EW("g", qS, qS_x1_done)
        for c in range(C):
            EI("g", lambda g, c=c: g.dma_start(
                wk_a1[:, c * D:(c + 1) * D], wqkv[c * 128:(c + 1) * 128, D:2 * D]), qG, 16)
        qg_wk = qG.v
        for c in range(C):
            EI("g", lambda g, c=c: g.dma_start(
                wv_a1[:, c * D:(c + 1) * D], wqkv[c * 128:(c + 1) * 128, 2 * D:3 * D]), qG, 16)
        qg_wv = qG.v

        EW("q", qS, qS_x1_done)
        EI("q", lambda q: q.dma_start(sb3(xa), xt3(x1h, 0)), qS, 16)
        qS_load_a1 = {0: qS.v}

        a1buf = lambda t: (B1, lnv0, lnv1) if t % 2 == 0 else (B1b, lnv0b, lnv1b)
        dv_xln_a1 = {}
        pe_v_last = {}
        pe_k_last = {}
        qS_xlh = {}
        dv_xln_cur = None
        ln_n = {}
        for t in range(NT):
            buf, v0, v1 = a1buf(t)
            if t == 0:
                dv_xln_cur = emit_ln(xa, [(qS, qS_load_a1[0])], buf, v0, v1)
            dv_xln_a1[t] = dv_xln_cur
            EW("q", dv, dv_xln_a1[t])
            EI("q", lambda q, t=t, buf=buf: q.dma_start(xt3(xlh, t * TT), sb3(buf)), qS, 16)
            qS_xlh[t] = qS.v
            if t + 1 < NT:
                EI("q", lambda q, t=t: q.dma_start(sb3(xa), xt3(x1h, (t + 1) * TT)), qS, 16)
                qS_load_a1[t + 1] = qS.v
            nxt = t + 1 < NT
            if nxt:
                bufn, v0n, v1n = a1buf(t + 1)
                pre = [(qS, qS_load_a1[t + 1])]
                if t >= 1:
                    pre.append((pe, pe_v_last[t - 1]))
                    pre.append((qS, qS_xlh[t - 1]))
                ac_sq_n = ln_sq(xa, bufn, pre)
            EW("t", dv, dv_xln_a1[t])
            if t == 0:
                EW("t", qG, qg_wk)
            for e in range(C):
                key = f"pb{2 + e % 2}"
                bwait("t", key)
                for c in range(C - 1):
                    E("t", lambda t_, e=e, c=c, buf=buf: t_.matmul(
                        pb[2 + e % 2][:, :],
                        wk_a1[:, c * D + e * 128: c * D + (e + 1) * 128],
                        buf[:, c * TT:(c + 1) * TT],
                        start=(c == 0), stop=False))
                pe_k = EI("t", lambda t_, e=e, buf=buf: t_.matmul(
                    pb[2 + e % 2][:, :],
                    wk_a1[:, (C - 1) * D + e * 128: (C - 1) * D + (e + 1) * 128],
                    buf[:, (C - 1) * TT:C * TT],
                    start=False, stop=True), pe)
                guards[key] = (pe, pe_k)
                EW("v", pe, pe_k)
                EI("v", lambda v, e=e, t=t: v.tensor_scalar(
                    bigA[:, e * S + t * TT: e * S + (t + 1) * TT], pb[2 + e % 2][:, :],
                    bias_cols[:, 40 + e:41 + e], None, AO.add), dv)
                bset(key, dv)
                if nxt and e == 2:
                    pe_ssq_n = ln_ssq(bufn, ac_sq_n)
                    ac_cast_n = ln_cast(xa, bufn, pe_ssq_n)
                if nxt and e == 5:
                    pe_sum_n = ln_sum(bufn, ac_cast_n)
                    ac_rstd_n = ln_stats2(v0n, v1n, pe_sum_n)
            pe_k_last[t] = pe.v
            if nxt:
                dv_xln_cur = ln_xln(xa, bufn, ac_rstd_n)
            if t == 0:
                EW("t", qG, qg_wv)
            for tc in range(4):
                for j in range(2):
                    key = f"pb{4 + j}"
                    bwait("t", key)
                    for c in range(C):
                        E("t", lambda t_, tc=tc, j=j, c=c, buf=buf: t_.matmul(
                            pb[4 + j][:, :],
                            buf[:, c * TT + tc * 128: c * TT + (tc + 1) * 128],
                            wv_a1[:, c * D + j * 512: c * D + (j + 1) * 512],
                            start=(c == 0), stop=False))
                    pe_v = EI("t", lambda t_, j=j: t_.matmul(
                        pb[4 + j][:, :], ones_row[0:1, 0:128],
                        bvrow[0:1, j * 512:(j + 1) * 512],
                        start=False, stop=True), pe)
                    guards[key] = (pe, pe_v)
                    EW("a", pe, pe_v)
                    EI("a", lambda a, t=t, tc=tc, j=j: a.activation(
                        bigB[:, (t * 4 + tc) * D + j * 512: (t * 4 + tc) * D + (j + 1) * 512],
                        pb[4 + j][:, :], AF.Copy), ac)
                    bset(key, ac)
            pe_v_last[t] = pe.v

        EW("g", pe, pe_k_last[NT - 1])
        for c in range(C):
            EI("g", lambda g, c=c: g.dma_start(
                wk_a1[:, c * D:(c + 1) * D], wqkv[c * 128:(c + 1) * 128, 0:D]), qG, 16)
        qg_wq = qG.v
        EW("g", pe, pe_v_last[NT - 1])
        EW("g", qS, qS_xlh[NT - 1])
        for c in range(C):
            EI("g", lambda g, c=c: g.dma_start(
                wqp[:, c * D:(c + 1) * D], wproj[c * 128:(c + 1) * 128, :]), qG, 16)
        qg_wp = qG.v

        qstage4 = qstage[:, :].rearrange("p (s e q) -> p s e q", s=4, e=C)
        qS_qh = {}
        dv_qev_last = {}
        pe_q_last = {}
        qS_b1q = {}
        EW("q", qS, qS_xlh[NT - 1])
        EW("q", pe, pe_v_last[NT - 1])
        for t in range(2 if QPP else 1):
            qbuf = (B1 if t % 2 == 0 else b1q) if QPP else B1
            EI("q", lambda q, t=t, qbuf=qbuf: q.dma_start(sb3(qbuf), xt3(xlh, t * TT)), qS, 16)
            qS_b1q[t] = qS.v
        for t in range(NT):
            qbuf = (B1 if t % 2 == 0 else b1q) if QPP else B1
            if not QPP and t >= 1:
                EW("q", qS, qS_xlh[t])
                EW("q", pe, pe_q_last[t - 1])
                EI("q", lambda q, t=t: q.dma_start(sb3(B1), xt3(xlh, t * TT)), qS, 16)
                qS_b1q[t] = qS.v
            EW("t", qS, qS_b1q[t])
            if t == 0:
                EW("t", qG, qg_wq)
                EW("t", dv, dv_xln_a1[NT - 1])
            for e in range(C):
                key = f"pb{2 + e % 2}"
                bwait("t", key)
                for c in range(C - 1):
                    E("t", lambda t_, e=e, c=c, qbuf=qbuf: t_.matmul(
                        pb[2 + e % 2][:, :],
                        wk_a1[:, c * D + e * 128: c * D + (e + 1) * 128],
                        qbuf[:, c * TT:(c + 1) * TT],
                        start=(c == 0), stop=False))
                pe_q = EI("t", lambda t_, e=e, qbuf=qbuf: t_.matmul(
                    pb[2 + e % 2][:, :],
                    wk_a1[:, (C - 1) * D + e * 128: (C - 1) * D + (e + 1) * 128],
                    qbuf[:, (C - 1) * TT:C * TT],
                    start=False, stop=True), pe)
                guards[key] = (pe, pe_q)
                EW("v", pe, pe_q)
                if t > 0 and e == 0:
                    EW("v", qS, qS_qh[(t - 1, 3)])
                EI("v", lambda v, e=e: v.tensor_scalar(
                    qstage4[:, :, e, :], pb[2 + e % 2][:, :],
                    bias_cols[:, 32 + e:33 + e], None, AO.add), dv)
                bset(key, dv)
            pe_q_last[t] = pe.v
            dv_qev_last[t] = dv.v
            for sub in range(4):
                EW("q", dv, dv_qev_last[t])
                EI("q", lambda q, t=t, sub=sub: q.dma_start(
                    qh[t * 4 + sub], qstage4[:, sub]), qS, 16)
                qS_qh[(t, sub)] = qS.v
            if QPP and t + 2 < NT:
                qbuf2 = B1 if t % 2 == 0 else b1q
                EW("q", qS, qS_xlh[t + 2])
                EW("q", pe, pe_q_last[t])
                EI("q", lambda q, t=t, qbuf2=qbuf2: q.dma_start(
                    sb3(qbuf2), xt3(xlh, (t + 2) * TT)), qS, 16)
                qS_b1q[t + 2] = qS.v


    if stage == 2:
        EW("g", dv, dv_qev_last[NT - 1])
        EW("g", qS, qS.v)
        EI("g", lambda g: g.dma_start(y.rearrange("(a p) t -> p a t", p=128), bigA[:, :].rearrange("p (a t) -> p a t", a=8)), qG, 16)
        EW("g", qG, qG.v)
        EW("q", qS, qS.v)

    if stage >= 3:
        barrier()

        bp_row = carve("bp_row_a2", [1, D], BF16, 51200)
        EI("g", lambda g: g.dma_start(bp_row[0:1, :], bproj[None, :]), qG, 16)
        qg_bpr = qG.v

        qgs = [qg0, qg1]
        qS_qg = {}
        EW("q", qS, qS_qh[(NT - 1, 3)])
        for qi in range(2):
            EI("q", lambda q, qi=qi: q.dma_start(
                qgs[qi % 2][:, :], qh[qi].rearrange("p c q -> p (c q)")), qS, 16)
            qS_qg[qi] = qS.v

        dv_z = {}
        ac_exp = {}
        ac_dl = {}
        qS_dl = {}
        pe_av = {}
        pe_pj = {}
        pe_sc_last = {}
        ac_sev_last = {}

        def a2_scores(qt):
            qg = qgs[qt % 2]
            EW("t", qS, qS_qg[qt])
            for kb in range(8):
                key = f"pb{2 + kb % 4}"
                bwait("t", key)
                for c in range(C - 1):
                    E("t", lambda t_, kb=kb, c=c, qg=qg: t_.matmul(
                        pb[2 + kb % 4][:, :],
                        qg[:, c * 128:(c + 1) * 128],
                        bigA[:, c * S + kb * 512: c * S + (kb + 1) * 512],
                        start=(c == 0), stop=False))
                pe_sc = EI("t", lambda t_, kb=kb, qg=qg: t_.matmul(
                    pb[2 + kb % 4][:, :],
                    qg[:, (C - 1) * 128:C * 128],
                    bigA[:, (C - 1) * S + kb * 512: (C - 1) * S + (kb + 1) * 512],
                    start=False, stop=True), pe)
                guards[key] = (pe, pe_sc)
                EW("a", pe, pe_sc)
                if kb == 0 and qt > 0:
                    EW("a", dv, dv_z[qt - 1])
                EI("a", lambda a, kb=kb: a.activation(
                    s_sb[:, kb * 512:(kb + 1) * 512], pb[2 + kb % 4][:, :], AF.Copy,
                    scale=0.125), ac)
                bset(key, ac)
            pe_sc_last[qt] = pe.v
            ac_sev_last[qt] = ac.v
            if qt + 2 < NQ:
                EW("q", pe, pe_sc_last[qt])
                EI("q", lambda q, qt=qt: q.dma_start(
                    qgs[qt % 2][:, :], qh[qt + 2].rearrange("p c q -> p (c q)")), qS, 16)
                qS_qg[qt + 2] = qS.v

        def a2_chain(qt):
            EW("v", ac, ac_sev_last[qt])
            if qt > 0:
                EW("v", ac, ac_exp[qt - 1])
            E("v", lambda v: v.tensor_scalar(work[:, :], s_sb[:, :], T4, K27, AO.is_lt, AO.mult))
            E("v", lambda v: v.scalar_tensor_tensor(work[:, :], s_sb[:, :], T3, work[:, :], AO.is_lt, AO.add))
            E("v", lambda v: v.tensor_scalar(work[:, :], work[:, :], -1.4, 2.0, AO.mult, AO.add))
            E("v", lambda v: v.scalar_tensor_tensor(work[:, :], s_sb[:, :], T2, work[:, :], AO.is_gt, AO.add))
            E("v", lambda v: v.scalar_tensor_tensor(work[:, :], s_sb[:, :], T1, work[:, :], AO.is_gt, AO.add))
            dv_z[qt] = EI("v", lambda v: v.tensor_tensor(
                work[:, :], work[:, :], s_sb[:, :], AO.mult), dv)

        def a2_rest(qt):
            q0 = qt * 128
            dv_mx = EI("v", lambda v: v.reduce_max(
                smalls[:, 0:1], work[:, :], axis=AX.X, negate=True), dv)
            EW("v", dv, dv_mx)
            dv_nm = EI("v", lambda v: v.tensor_scalar(
                smalls[:, 1:2], smalls[:, 0:1], 0.5, None, AO.mult), dv)
            EW("a", dv, dv_nm)
            if qt > 0:
                EW("a", pe, pe_av[qt - 1])
            ac_exp[qt] = EI("a", lambda a: a.activation(
                p_sb[:, :], work[:, :], AF.Exp, bias=smalls[:, 1:2], scale=0.5,
                accum_out=smalls[:, 2:3]), ac)
            EW("v", ac, ac_exp[qt])
            dv_rl = EI("v", lambda v: v.reciprocal(smalls[:, 3:4], smalls[:, 2:3]), dv)
            EW("t", ac, ac_exp[qt])
            bwait("t", "pb0")
            bwait("t", "pb1")
            ac_pT = {}

            def t_blk(blk):
                key = f"pt{blk % 2}"
                bwait("t", key)
                for j in range(7):
                    kc = blk * 8 + j
                    E("t", lambda t_, blk=blk, kc=kc: t_.transpose(
                        pt[blk % 2][:, (kc % 8) * 128:(kc % 8 + 1) * 128],
                        p_sb[:, kc * 128:(kc + 1) * 128], ident[:, :]))
                kc = blk * 8 + 7
                pe_T = EI("t", lambda t_, blk=blk, kc=kc: t_.transpose(
                    pt[blk % 2][:, (kc % 8) * 128:(kc % 8 + 1) * 128],
                    p_sb[:, kc * 128:(kc + 1) * 128], ident[:, :]), pe)
                guards[key] = (pe, pe_T)
                EW("a", pe, pe_T)
                ac_pT[blk] = EI("a", lambda a, blk=blk: a.activation(
                    p_sb[:, blk * 1024:(blk + 1) * 1024], pt[blk % 2][:, :], AF.Copy), ac)
                bset(key, ac)

            def av_blk(blk, last):
                EW("t", ac, ac_pT[blk])
                for j in range(8):
                    kc = blk * 8 + j
                    for h in range(2):
                        if last and kc == 31 and h == 1:
                            continue
                        E("t", lambda t_, kc=kc, h=h: t_.matmul(
                            pb[h][:, :],
                            p_sb[:, kc * 128:(kc + 1) * 128],
                            bigB[:, kc * D + h * 512: kc * D + (h + 1) * 512],
                            start=(kc == 0), stop=(kc == 31)))

            t_blk(0)
            t_blk(1)
            av_blk(0, False)
            t_blk(2)
            av_blk(1, False)
            t_blk(3)
            av_blk(2, False)
            av_blk(3, True)
            pe_av[qt] = EI("t", lambda t_: t_.matmul(
                pb[1][:, :], p_sb[:, 31 * 128:32 * 128],
                bigB[:, 31 * D + 512: 32 * D],
                start=False, stop=True), pe)
            guards["pb0"] = (pe, pe_av[qt])
            guards["pb1"] = (pe, pe_av[qt])
            EW("a", pe, pe_av[qt])
            EW("a", dv, dv_rl)
            if qt > 0:
                EW("a", pe, pe_pj[qt - 1])
            E("a", lambda a: a.activation(
                aout[:, 0:512], pb[0][:, :], AF.Copy, scale=smalls[:, 3:4]))
            ac_ao = EI("a", lambda a: a.activation(
                aout[:, 512:1024], pb[1][:, :], AF.Copy, scale=smalls[:, 3:4]), ac)
            bset("pb0", ac)
            bset("pb1", ac)
            EW("t", ac, ac_ao)
            bwait("t", "pt0")
            for j in range(7):
                E("t", lambda t_, j=j: t_.transpose(
                    pt[0][:, j * 128:(j + 1) * 128], aout[:, j * 128:(j + 1) * 128], ident[:, :]))
            pe_aT = EI("t", lambda t_: t_.transpose(
                pt[0][:, 7 * 128:8 * 128], aout[:, 7 * 128:8 * 128], ident[:, :]), pe)
            guards["pt0"] = (pe, pe_aT)
            EW("a", pe, pe_aT)
            ac_aT = EI("a", lambda a: a.activation(aout[:, :], pt[0][:, :], AF.Copy), ac)
            bset("pt0", ac)
            EW("t", ac, ac_aT)
            if qt == 0:
                EW("t", qG, qg_wp)
                EW("t", qG, qg_bpr)
            bwait("t", "pb0")
            bwait("t", "pb1")
            for e in range(C):
                E("t", lambda t_, e=e: t_.matmul(
                    pb[e // 4][:, (e % 4) * 128:(e % 4 + 1) * 128],
                    bp_row[0:1, e * 128:(e + 1) * 128],
                    ones_row[0:1, 0:128],
                    start=True, stop=False))
                for c in range(C):
                    if e == C - 1 and c == C - 1:
                        continue
                    E("t", lambda t_, e=e, c=c: t_.matmul(
                        pb[e // 4][:, (e % 4) * 128:(e % 4 + 1) * 128],
                        wqp[:, c * D + e * 128: c * D + (e + 1) * 128],
                        aout[:, c * 128:(c + 1) * 128],
                        start=False, stop=(c == C - 1)))
            pe_pj[qt] = EI("t", lambda t_: t_.matmul(
                pb[1][:, 3 * 128:4 * 128],
                wqp[:, (C - 1) * D + (C - 1) * 128: (C - 1) * D + C * 128],
                aout[:, (C - 1) * 128:C * 128],
                start=False, stop=True), pe)
            guards["pb0"] = (pe, pe_pj[qt])
            guards["pb1"] = (pe, pe_pj[qt])
            EW("a", pe, pe_pj[qt])
            if qt > 0:
                EW("a", qS, qS_dl[qt - 1])
            E("a", lambda a: a.activation(delta[:, 0:512], pb[0][:, :], AF.Copy))
            ac_dl[qt] = EI("a", lambda a: a.activation(delta[:, 512:1024], pb[1][:, :], AF.Copy), ac)
            bset("pb0", ac)
            bset("pb1", ac)
            EW("q", ac, ac_dl[qt])
            dl_dst = dlh if stage >= 4 else y
            EI("q", lambda q, q0=q0, dl_dst=dl_dst: q.dma_start(
                dl_dst.rearrange("(c p) t -> p c t", p=128)[:, :, q0:q0 + 128],
                delta[:, :].rearrange("p (c q) -> p c q", c=C)), qS, 16)
            qS_dl[qt] = qS.v

        a2_scores(0)
        for qt in range(NQ):
            a2_chain(qt)
            if qt + 1 < NQ:
                a2_scores(qt + 1)
            a2_rest(qt)

        EW("g", pe, pe_sc_last[NQ - 1])
        for c in range(C):
            EI("g", lambda g, c=c: g.dma_start(
                bigA[:, c * WF:(c + 1) * WF], w1[c * 128:(c + 1) * 128, :]), qG, 16)
        qg_w1_2 = qG.v
        EW("g", pe, pe_av[NQ - 1])
        for w in range(WC):
            EI("g", lambda g, w=w: g.dma_start(
                bigB[:, w * D:(w + 1) * D], w2[w * 128:(w + 1) * 128, :]), qG, 16)
        qg_w2_2 = qG.v

    if stage >= 4:
        barrier()

        acc_gate = {}
        for t0_ in range(2):
            tgt = xa if t0_ % 2 == 0 else xa2
            EI("q", lambda q, t0_=t0_, tgt=tgt: q.dma_start(
                sb3(tgt), xt3(x1h, t0_ * TT)), qS, 16)
            qS_load[t0_] = qS.v
            EW("g", qS, qS_load[t0_])
            EI("g", lambda g, t0_=t0_, tgt=tgt: g.dma_start(
                sb3(tgt), xt3(dlh, t0_ * TT), accum_op=AO.add), qG, 16)
            acc_gate[t0_] = qG.v
        emit_ffn(True, qg_w1_2, qg_w2_2, acc_gate)

    EW("q", qS, qS.v)

    with nc.Block() as block:
        @block.tensor
        def _(eng):
            for fn in ops["t"]:
                fn(eng)

        @block.scalar
        def _(eng):
            for fn in ops["a"]:
                fn(eng)

        @block.vector
        def _(eng):
            for fn in ops["v"]:
                fn(eng)

        @block.gpsimd
        def _(eng):
            for fn in ops["g"]:
                fn(eng)

        @block.sync
        def _(eng):
            for fn in ops["q"]:
                fn(eng)

    return nc


_NC_CACHE = None


def kernel(L0_attn_x, L0_attn_W_qkv, L0_attn_b_qkv, L0_attn_W_proj, L0_attn_b_proj,
           L0_ffn_w1, L0_ffn_b1, L0_ffn_w2, L0_ffn_b2):
    global _NC_CACHE
    xs = np.asarray(L0_attn_x, dtype=np.float32)
    B = xs.shape[0]
    if _NC_CACHE is None:
        _NC_CACHE = build_nc()
    nc = _NC_CACHE
    common = {
        "wqkv": np.ascontiguousarray(L0_attn_W_qkv, dtype=np.float32),
        "bqkv": np.ascontiguousarray(L0_attn_b_qkv, dtype=np.float32),
        "wproj": np.ascontiguousarray(L0_attn_W_proj, dtype=np.float32),
        "bproj": np.ascontiguousarray(L0_attn_b_proj, dtype=np.float32),
        "w1": np.ascontiguousarray(L0_ffn_w1, dtype=np.float32),
        "b1": np.ascontiguousarray(L0_ffn_b1, dtype=np.float32),
        "w2": np.ascontiguousarray(L0_ffn_w2, dtype=np.float32),
        "b2": np.ascontiguousarray(L0_ffn_b2, dtype=np.float32),
    }
    in_maps = []
    for c in range(B):
        m = dict(common)
        m["x"] = np.ascontiguousarray(xs[c].T)
        in_maps.append(m)
    res = run_bass_kernel_spmd(nc, in_maps, core_ids=list(range(B)))
    out = np.stack([np.asarray(r["y"]).T for r in res.results], axis=0)
    return out.astype(np.float32)


# version 09
# speedup vs baseline: 1.0090x; 1.0090x over previous
import math
import numpy as np

import concourse.bass as bass
import concourse.mybir as mybir
from concourse.bass_utils import run_bass_kernel_spmd
from concourse.alu_op_type import AluOpType as AO

F32 = mybir.dt.float32
BF16 = mybir.dt.bfloat16
U8 = mybir.dt.uint8
AF = mybir.ActivationFunctionType
AX = mybir.AxisListType

D = 1024
S = 4096
C = 8
TT = 512
NT = S // TT
WF = 4096
WC = 32
NQ = 32
EPS = 1e-5
T1 = math.log(0.95 / 0.05)
T2 = math.log(0.85 / 0.15)
T3 = math.log(0.25 / 0.75)
T4 = math.log(0.10 / 0.90)
K27 = 2.0 / 7.0

SCRATCH_BYTES = 75776
import os
QPP = os.environ.get("K_QPP", "1") == "1"
A1IL = os.environ.get("K_A1IL", "1") == "1"



def build_nc(stage=4):
    nc = bass.Bass("TRN2", target_bir_lowering=False, detect_race_conditions=False)

    x = nc.dram_tensor("x", [D, S], F32, kind="ExternalInput").ap()
    wqkv = nc.dram_tensor("wqkv", [D, 3 * D], F32, kind="ExternalInput").ap()
    bqkv = nc.dram_tensor("bqkv", [3 * D], F32, kind="ExternalInput").ap()
    wproj = nc.dram_tensor("wproj", [D, D], F32, kind="ExternalInput").ap()
    bproj = nc.dram_tensor("bproj", [D], F32, kind="ExternalInput").ap()
    w1 = nc.dram_tensor("w1", [D, WF], F32, kind="ExternalInput").ap()
    b1 = nc.dram_tensor("b1", [WF], F32, kind="ExternalInput").ap()
    w2 = nc.dram_tensor("w2", [WF, D], F32, kind="ExternalInput").ap()
    b2 = nc.dram_tensor("b2", [D], F32, kind="ExternalInput").ap()
    y = nc.dram_tensor("y", [D, S], F32, kind="ExternalOutput").ap()
    dbg = os.environ.get("K_DBG", "0") == "1"
    ik = dict(kind="ExternalOutput") if dbg else {}
    x1h = nc.dram_tensor("x1h", [D, S], F32, **ik).ap()
    dlh = nc.dram_tensor("dlh", [D, S], F32, **ik).ap()
    xlh = nc.dram_tensor("xlh", [D, S], BF16, **ik).ap()
    qh = nc.dram_tensor("qh", [NQ, 128, C, 128], BF16, **ik).ap()

    scratch = nc.alloc_sbuf_tensor("scratch", [128, SCRATCH_BYTES], U8)
    base = nc.lookup_mloc(scratch).addr

    def carve(name, shape, dtype, off):
        h = nc.virtual_sbuf_tensor(name, shape, dtype)
        nc._tensor(h.name, h.shape, h.dtype, type="SB")
        mloc = nc.lookup_mloc(h)
        mloc.allocated = True
        mloc.addr = base + off
        return h

    xa = carve("xa", [128, C * TT], F32, 0)
    xa2 = carve("xa2", [128, C * TT], F32, 16384)
    B1 = carve("B1", [128, C * TT], BF16, 32768)
    lnv0 = carve("lnv0", [128, TT], F32, 32768)
    lnv1 = carve("lnv1", [128, TT], F32, 34816)
    t1s = carve("t1s", [128, TT], F32, 40960)
    h_sb = carve("h_sb", [128, WC * TT], BF16, 43008)
    wk_a1 = carve("wk_a1", [128, C * D], BF16, 16384)
    wv_a1 = carve("wv_a1", [128, C * D], BF16, 43008)
    qstage = carve("qstage", [128, C * TT], BF16, 0)
    s_sb = carve("s_sb", [128, S], F32, 0)
    work = carve("work", [128, S], F32, 16384)
    p_sb = carve("p_sb", [128, S], BF16, 32768)
    qg0 = carve("qg0", [128, C * 128], BF16, 40960)
    qg1 = carve("qg1", [128, C * 128], BF16, 43008)
    delta = carve("delta", [128, C * 128], F32, 45056)
    aout = carve("aout", [128, D], BF16, 49152)
    wqp = carve("wqp", [128, C * D], BF16, 59392)

    bigA = nc.alloc_sbuf_tensor("bigA", [128, C * S], BF16)
    bigB = nc.alloc_sbuf_tensor("bigB", [128, WC * D], BF16)
    ident = nc.alloc_sbuf_tensor("ident", [128, 128], BF16)
    inv_bc = nc.alloc_sbuf_tensor("inv_bc", [128, 128], BF16)
    ones_row = nc.alloc_sbuf_tensor("ones_row", [1, TT], BF16)
    bvrow = nc.alloc_sbuf_tensor("bvrow", [1, D], BF16)
    bias_cols = nc.alloc_sbuf_tensor("bias_cols", [128, 64], F32)
    smalls = nc.alloc_sbuf_tensor("smalls", [128, 8], F32)

    pb = [nc.alloc_psum_tensor(f"pb{i}", [128, 512], F32) for i in range(6)]
    pt = [nc.alloc_psum_tensor(f"pt{i}", [128, 1024], BF16) for i in range(2)]

    ops = {k: [] for k in "tavgq"}

    class Sem:
        __slots__ = ("h", "v")

        def __init__(self, name):
            self.h = nc.semaphore(name).__enter__()
            self.v = 0

    qS = Sem("qS")
    qG = Sem("qG")
    pe = Sem("pe")
    ac = Sem("ac")
    dv = Sem("dv")
    pl = Sem("pl")
    SEMS = {"t": pe, "a": ac, "v": dv, "g": pl, "q": qS}

    def E(k, fn):
        ops[k].append(fn)

    def EW(k, sem, val):
        assert 0 < val <= sem.v, (k, val, sem.v)
        E(k, lambda e, h=sem.h, v=val: e.wait_ge(h, v))

    def EI(k, fn, sem, n=1):
        sem.v += n
        E(k, lambda e, fn=fn, h=sem.h, n=n: fn(e).then_inc(h, n))
        return sem.v

    guards = {}

    def bwait(k, key):
        if key in guards:
            s, v = guards[key]
            if v > 0:
                EW(k, s, v)

    def bset(key, sem):
        guards[key] = (sem, sem.v)

    def barrier(qg_val=None):
        snap = [(qS, qS.v), (qG, qG.v if qg_val is None else qg_val),
                (pe, pe.v), (ac, ac.v), (dv, dv.v), (pl, pl.v)]
        for k in "tavgq":
            own = SEMS[k]
            for sem, val in snap:
                if val > 0 and sem is not own:
                    EW(k, sem, val)

    def xt3(ap_hbm, t0, tt=TT):
        return ap_hbm.rearrange("(c p) t -> p c t", p=128)[:, :, t0:t0 + tt]

    def sb3(ten, c=C):
        return ten[:, :].rearrange("p (c t) -> p c t", c=c)

    def xsl(buf, c):
        return buf[:, c * TT:(c + 1) * TT]

    E("g", lambda g: g.memset(ident[:, :], 0.0))
    E("g", lambda g: g.affine_select(
        out=ident[:, :], in_=ident[:, :], compare_op=AO.not_equal,
        fill=1.0, base=0, pattern=[[-1, 128]], channel_multiplier=1))
    E("g", lambda g: g.memset(inv_bc[:, :], 1.0 / 1024.0))
    consts_done = EI("g", lambda g: g.memset(ones_row[0:1, :], 1.0), pl)

    st_b1 = h_sb[0:1, 0:WF]
    st_bq = h_sb[0:1, WF:WF + D]
    st_bk = h_sb[0:1, WF + D:WF + 2 * D]
    st_bp = h_sb[0:1, WF + 2 * D:WF + 3 * D]
    st_b2 = h_sb[0:1, WF + 3 * D:WF + 4 * D]
    EI("g", lambda g: g.dma_start(bvrow[0:1, :], bqkv[None, 2 * D:3 * D]), qG, 16)
    EI("g", lambda g: g.dma_start(st_b1, b1[None, :]), qG, 16)
    EI("g", lambda g: g.dma_start(st_bq, bqkv[None, 0:D]), qG, 16)
    EI("g", lambda g: g.dma_start(st_bk, bqkv[None, D:2 * D]), qG, 16)
    EI("g", lambda g: g.dma_start(st_bp, bproj[None, :]), qG, 16)
    qg_bias = EI("g", lambda g: g.dma_start(st_b2, b2[None, :]), qG, 16)
    for c in range(C):
        EI("g", lambda g, c=c: g.dma_start(
            bigA[:, c * WF:(c + 1) * WF], w1[c * 128:(c + 1) * 128, :]), qG, 16)
    qg_w1 = qG.v
    for w in range(WC):
        EI("g", lambda g, w=w: g.dma_start(
            bigB[:, w * D:(w + 1) * D], w2[w * 128:(w + 1) * 128, :]), qG, 16)
    qg_w2 = qG.v

    EW("t", pl, consts_done)
    EW("t", qG, qg_bias)
    bias_srcs = ([(st_b1, w, w) for w in range(WC)]
                 + [(st_bq, e, 32 + e) for e in range(C)]
                 + [(st_bk, e, 40 + e) for e in range(C)]
                 + [(st_bp, e, 48 + e) for e in range(C)]
                 + [(st_b2, e, 56 + e) for e in range(C)])
    for (stg, i, col) in bias_srcs[:-1]:
        E("t", lambda t, stg=stg, i=i, col=col: t.matmul(
            pb[0][:, col:col + 1], stg[0:1, i * 128:(i + 1) * 128],
            ones_row[0:1, 0:1], start=True, stop=True))
    stg_l, i_l, col_l = bias_srcs[-1]
    pe_bias = EI("t", lambda t, stg=stg_l, i=i_l, col=col_l: t.matmul(
        pb[0][:, col:col + 1], stg[0:1, i * 128:(i + 1) * 128],
        ones_row[0:1, 0:1], start=True, stop=True), pe)
    EW("a", pe, pe_bias)
    ac_bias = EI("a", lambda a: a.activation(bias_cols[:, 0:64], pb[0][:, 0:64], AF.Copy), ac)
    bset("pb0", ac)
    EW("v", ac, ac_bias)
    dv_hb2c = EI("v", lambda v: v.tensor_scalar(
        bias_cols[:, 56:64], bias_cols[:, 56:64], 0.5, None, AO.mult), dv)

    qS_load = {}
    EI("q", lambda q: q.dma_start(sb3(xa), xt3(x, 0)), qS, 16)
    qS_load[0] = qS.v
    EI("q", lambda q: q.dma_start(sb3(xa2), xt3(x, TT)), qS, 16)
    qS_load[1] = qS.v

    if stage == 0:
        y3 = y.rearrange("(a p) t -> p a t", p=128)
        EW("q", ac, ac_bias)
        EW("q", dv, dv_hb2c)
        EI("q", lambda q: q.dma_start(y3[:, 0, 0:64], bias_cols[:, :]), qS, 16)
        EW("g", qG, qG.v)
        EI("g", lambda g: g.dma_start(y3[:, 1, :], bigA[:, 0:4096]), qG, 16)
        EI("g", lambda g: g.dma_start(y3[:, 3, 0:128], ident[:, :]), qG, 16)
        EW("q", qS, qS_load[1])
        EI("q", lambda q: q.dma_start(y3[:, 2, :], xa[:, :]), qS, 16)
        EW("q", qS, qS.v)
        EW("g", qG, qG.v)

    def ln_sq(xin, buf, a_pre):
        for (s, v) in a_pre:
            EW("a", s, v)
        return EI("a", lambda a: a.activation(buf[:, :], xin[:, :], AF.Square), ac)

    def ln_ssq(buf, ac_sq):
        bwait("t", "pb0")
        EW("t", ac, ac_sq)
        for c in range(C - 1):
            E("t", lambda t, c=c, buf=buf: t.matmul(
                pb[0][:, :], inv_bc[:, :], buf[:, c * TT:(c + 1) * TT],
                start=(c == 0), stop=False))
        return EI("t", lambda t, buf=buf: t.matmul(
            pb[0][:, :], inv_bc[:, :], buf[:, (C - 1) * TT:C * TT],
            start=False, stop=True), pe)

    def ln_cast(xin, buf, pe_ssq):
        EW("a", pe, pe_ssq)
        return EI("a", lambda a: a.activation(buf[:, :], xin[:, :], AF.Copy), ac)

    def ln_sum(buf, ac_cast):
        bwait("t", "pb1")
        EW("t", ac, ac_cast)
        for c in range(C - 1):
            E("t", lambda t, c=c, buf=buf: t.matmul(
                pb[1][:, :], inv_bc[:, :], buf[:, c * TT:(c + 1) * TT],
                start=(c == 0), stop=False))
        return EI("t", lambda t, buf=buf: t.matmul(
            pb[1][:, :], inv_bc[:, :], buf[:, (C - 1) * TT:C * TT],
            start=False, stop=True), pe)

    def ln_stats2(v0, v1, pe_sum):
        EW("a", pe, pe_sum)
        ac_msq = EI("a", lambda a: a.activation(v0[:, :], pb[1][:, :], AF.Square), ac)
        EW("v", ac, ac_msq)
        dv_var = EI("v", lambda v: v.scalar_tensor_tensor(
            v1[:, :], pb[0][:, :], EPS, v0[:, :], AO.add, AO.subtract), dv)
        EW("a", dv, dv_var)
        E("a", lambda a: a.activation(v0[:, :], v1[:, :], AF.Ln))
        return EI("a", lambda a: a.activation(pb[0][:, :], v0[:, :], AF.Exp, scale=-0.5), ac)

    def ln_xln(xin, buf, ac_rstd):
        EW("v", ac, ac_rstd)
        for c in range(C):
            E("v", lambda v, c=c, xin=xin: v.tensor_tensor(
                t1s[:, :], xin[:, c * TT:(c + 1) * TT], pb[1][:, :], AO.subtract))
            if c < C - 1:
                E("v", lambda v, c=c, buf=buf: v.tensor_tensor(
                    buf[:, c * TT:(c + 1) * TT], t1s[:, :], pb[0][:, :], AO.mult))
        dv_xln = EI("v", lambda v, buf=buf: v.tensor_tensor(
            buf[:, (C - 1) * TT:C * TT], t1s[:, :], pb[0][:, :], AO.mult), dv)
        bset("pb0", dv)
        bset("pb1", dv)
        return dv_xln

    def emit_ln(xin, a_pre, buf=None, v0=None, v1=None):
        buf = B1 if buf is None else buf
        v0 = lnv0 if v0 is None else v0
        v1 = lnv1 if v1 is None else v1
        ac_sq = ln_sq(xin, buf, a_pre)
        pe_ssq = ln_ssq(buf, ac_sq)
        ac_cast = ln_cast(xin, buf, pe_ssq)
        pe_sum = ln_sum(buf, ac_cast)
        ac_rstd = ln_stats2(v0, v1, pe_sum)
        return ln_xln(xin, buf, ac_rstd)

    def emit_ffn(second, w1_gate, w2_gate, acc_gate=None):
        pe_hlast_prev = None
        qS_out_end = {}
        ln_state = {}
        dv_xln_cur = None

        def ffn_halfb2(xin):
            for c in range(C):
                E("v", lambda v, c=c, xin=xin: v.tensor_scalar(
                    xsl(xin, c), xsl(xin, c), bias_cols[:, 56 + c:57 + c], None, AO.add))

        for t in range(NT):
            xin = xa if t % 2 == 0 else xa2
            if t == 0:
                gate = (qS, qS_load[0]) if not second else (qG, acc_gate[0])
                dv_xln_cur = emit_ln(xin, [gate])
                ffn_halfb2(xin)
            EW("t", dv, dv_xln_cur)
            if t == 0:
                EW("t", qG, w1_gate)
            pe_h = {}
            for w in range(WC):
                key = f"pb{2 + w % 2}"
                bwait("t", key)
                for c in range(C - 1):
                    E("t", lambda t_, w=w, c=c: t_.matmul(
                        pb[2 + w % 2][:, :],
                        bigA[:, c * WF + w * 128: c * WF + (w + 1) * 128],
                        B1[:, c * TT:(c + 1) * TT],
                        start=(c == 0), stop=False))
                pe_h[w] = EI("t", lambda t_, w=w: t_.matmul(
                    pb[2 + w % 2][:, :],
                    bigA[:, (C - 1) * WF + w * 128: (C - 1) * WF + (w + 1) * 128],
                    B1[:, (C - 1) * TT:C * TT],
                    start=False, stop=True), pe)
                guards[key] = (pe, pe_h[w])
                EW("a", pe, pe_h[w])
                EI("a", lambda a, w=w: a.activation(
                    h_sb[:, w * TT:(w + 1) * TT], pb[2 + w % 2][:, :], AF.Relu,
                    bias=bias_cols[:, w:w + 1]), ac)
                bset(f"pb{2 + w % 2}", ac)
            ac_hev_last = ac.v
            pe_hlast_prev = pe_h[WC - 1]
            nxt = t + 1 < NT
            xnx = (xa if (t + 1) % 2 == 0 else xa2) if nxt else None
            if nxt:
                gate = (qS, qS_load[t + 1]) if not second else (qG, acc_gate[t + 1])
                ac_sq_n = ln_sq(xnx, B1, [gate, (pe, pe_hlast_prev)])
            EW("t", ac, ac_hev_last)
            if t == 0:
                EW("t", qG, w2_gate)
            for d in range(C):
                key = f"pb{4 + d % 2}"
                bwait("t", key)
                for w in range(WC - 1):
                    E("t", lambda t_, d=d, w=w: t_.matmul(
                        pb[4 + d % 2][:, :],
                        bigB[:, w * D + d * 128: w * D + (d + 1) * 128],
                        h_sb[:, w * TT:(w + 1) * TT],
                        start=(w == 0), stop=False))
                pe_y = EI("t", lambda t_, d=d: t_.matmul(
                    pb[4 + d % 2][:, :],
                    bigB[:, (WC - 1) * D + d * 128: (WC - 1) * D + (d + 1) * 128],
                    h_sb[:, (WC - 1) * TT:WC * TT],
                    start=False, stop=True), pe)
                guards[key] = (pe, pe_y)
                EW("v", pe, pe_y)
                dv_ye = EI("v", lambda v, d=d, xin=xin: v.scalar_tensor_tensor(
                    xsl(xin, d), pb[4 + d % 2][:, :], 0.5, xsl(xin, d),
                    AO.mult, AO.add), dv)
                bset(key, dv)
                dst = (x1h if stage > 1 else y) if not second else y
                EW("q", dv, dv_ye)
                EI("q", lambda q, d=d, t=t, dst=dst, xin=xin: q.dma_start(
                    dst[d * 128:(d + 1) * 128, t * TT:(t + 1) * TT], xsl(xin, d)), qS, 16)
                if nxt and d == 1:
                    pe_ssq_n = ln_ssq(B1, ac_sq_n)
                    ac_cast_n = ln_cast(xnx, B1, pe_ssq_n)
                if nxt and d == 3:
                    pe_sum_n = ln_sum(B1, ac_cast_n)
                    ac_rstd_n = ln_stats2(lnv0, lnv1, pe_sum_n)
                if nxt and d == 4:
                    dv_xln_cur = ln_xln(xnx, B1, ac_rstd_n)
                    ffn_halfb2(xnx)
            qS_out_end[t] = qS.v
            if t + 2 < NT:
                EW("q", qS, qS_out_end[t])
                tgt = xa if (t + 2) % 2 == 0 else xa2
                if not second:
                    EI("q", lambda q, t=t, tgt=tgt: q.dma_start(
                        sb3(tgt), xt3(x, (t + 2) * TT)), qS, 16)
                    qS_load[t + 2] = qS.v
                else:
                    EI("q", lambda q, t=t, tgt=tgt: q.dma_start(
                        sb3(tgt), xt3(x1h, (t + 2) * TT)), qS, 16)
                    qS_load[t + 2] = qS.v
                    EW("g", qS, qS_load[t + 2])
                    EI("g", lambda g, t=t, tgt=tgt: g.dma_start(
                        sb3(tgt), xt3(dlh, (t + 2) * TT), accum_op=AO.add), qG, 16)
                    acc_gate[t + 2] = qG.v
        return qS_out_end[NT - 1]

    snap0 = [(qS, qS.v), (pe, pe.v), (ac, ac.v), (dv, dv.v), (pl, pl.v)]
    for k_ in "tavgq":
        own_ = SEMS[k_]
        for sem_, val_ in snap0:
            if val_ > 0 and sem_ is not own_:
                EW(k_, sem_, val_)
    if stage >= 1:
        qS_x1_done = emit_ffn(False, qg_w1, qg_w2)

    if stage >= 2:
        barrier()

        B1b = carve("B1b", [128, C * TT], BF16, 59392)
        lnv0b = carve("lnv0b", [128, TT], F32, 59392)
        lnv1b = carve("lnv1b", [128, TT], F32, 61440)
        b1q = carve("b1q", [128, C * TT], BF16, 43008)

        EW("g", qS, qS_x1_done)
        for c in range(C):
            EI("g", lambda g, c=c: g.dma_start(
                wk_a1[:, c * D:(c + 1) * D], wqkv[c * 128:(c + 1) * 128, D:2 * D]), qG, 16)
        qg_wk = qG.v
        for c in range(C):
            EI("g", lambda g, c=c: g.dma_start(
                wv_a1[:, c * D:(c + 1) * D], wqkv[c * 128:(c + 1) * 128, 2 * D:3 * D]), qG, 16)
        qg_wv = qG.v

        EW("q", qS, qS_x1_done)
        EI("q", lambda q: q.dma_start(sb3(xa), xt3(x1h, 0)), qS, 16)
        qS_load_a1 = {0: qS.v}

        a1buf = lambda t: (B1, lnv0, lnv1) if t % 2 == 0 else (B1b, lnv0b, lnv1b)
        dv_xln_a1 = {}
        pe_v_last = {}
        pe_k_last = {}
        qS_xlh = {}
        dv_xln_cur = None
        ln_n = {}
        for t in range(NT):
            buf, v0, v1 = a1buf(t)
            if t == 0:
                dv_xln_cur = emit_ln(xa, [(qS, qS_load_a1[0])], buf, v0, v1)
            dv_xln_a1[t] = dv_xln_cur
            EW("q", dv, dv_xln_a1[t])
            EI("q", lambda q, t=t, buf=buf: q.dma_start(xt3(xlh, t * TT), sb3(buf)), qS, 16)
            qS_xlh[t] = qS.v
            if t + 1 < NT:
                EI("q", lambda q, t=t: q.dma_start(sb3(xa), xt3(x1h, (t + 1) * TT)), qS, 16)
                qS_load_a1[t + 1] = qS.v
            nxt = t + 1 < NT
            if nxt:
                bufn, v0n, v1n = a1buf(t + 1)
                pre = [(qS, qS_load_a1[t + 1])]
                if t >= 1:
                    pre.append((pe, pe_v_last[t - 1]))
                    pre.append((qS, qS_xlh[t - 1]))
                ac_sq_n = ln_sq(xa, bufn, pre)
            EW("t", dv, dv_xln_a1[t])
            if t == 0:
                EW("t", qG, qg_wk)
            for e in range(C):
                key = f"pb{2 + e % 2}"
                bwait("t", key)
                for c in range(C - 1):
                    E("t", lambda t_, e=e, c=c, buf=buf: t_.matmul(
                        pb[2 + e % 2][:, :],
                        wk_a1[:, c * D + e * 128: c * D + (e + 1) * 128],
                        buf[:, c * TT:(c + 1) * TT],
                        start=(c == 0), stop=False))
                pe_k = EI("t", lambda t_, e=e, buf=buf: t_.matmul(
                    pb[2 + e % 2][:, :],
                    wk_a1[:, (C - 1) * D + e * 128: (C - 1) * D + (e + 1) * 128],
                    buf[:, (C - 1) * TT:C * TT],
                    start=False, stop=True), pe)
                guards[key] = (pe, pe_k)
                EW("v", pe, pe_k)
                EI("v", lambda v, e=e, t=t: v.tensor_scalar(
                    bigA[:, e * S + t * TT: e * S + (t + 1) * TT], pb[2 + e % 2][:, :],
                    bias_cols[:, 40 + e:41 + e], None, AO.add), dv)
                bset(key, dv)
                if nxt and e == 2:
                    pe_ssq_n = ln_ssq(bufn, ac_sq_n)
                    ac_cast_n = ln_cast(xa, bufn, pe_ssq_n)
                if nxt and e == 5:
                    pe_sum_n = ln_sum(bufn, ac_cast_n)
                    ac_rstd_n = ln_stats2(v0n, v1n, pe_sum_n)
            pe_k_last[t] = pe.v
            if nxt:
                dv_xln_cur = ln_xln(xa, bufn, ac_rstd_n)
            if t == 0:
                EW("t", qG, qg_wv)
            for tc in range(4):
                for j in range(2):
                    key = f"pb{4 + j}"
                    bwait("t", key)
                    for c in range(C):
                        E("t", lambda t_, tc=tc, j=j, c=c, buf=buf: t_.matmul(
                            pb[4 + j][:, :],
                            buf[:, c * TT + tc * 128: c * TT + (tc + 1) * 128],
                            wv_a1[:, c * D + j * 512: c * D + (j + 1) * 512],
                            start=(c == 0), stop=False))
                    pe_v = EI("t", lambda t_, j=j: t_.matmul(
                        pb[4 + j][:, :], ones_row[0:1, 0:128],
                        bvrow[0:1, j * 512:(j + 1) * 512],
                        start=False, stop=True), pe)
                    guards[key] = (pe, pe_v)
                    EW("a", pe, pe_v)
                    EI("a", lambda a, t=t, tc=tc, j=j: a.activation(
                        bigB[:, (t * 4 + tc) * D + j * 512: (t * 4 + tc) * D + (j + 1) * 512],
                        pb[4 + j][:, :], AF.Copy), ac)
                    bset(key, ac)
            pe_v_last[t] = pe.v

        EW("g", pe, pe_k_last[NT - 1])
        for c in range(C):
            EI("g", lambda g, c=c: g.dma_start(
                wk_a1[:, c * D:(c + 1) * D], wqkv[c * 128:(c + 1) * 128, 0:D]), qG, 16)
        qg_wq = qG.v
        EW("g", pe, pe_v_last[NT - 1])
        EW("g", qS, qS_xlh[NT - 1])
        for c in range(C):
            EI("g", lambda g, c=c: g.dma_start(
                wqp[:, c * D:(c + 1) * D], wproj[c * 128:(c + 1) * 128, :]), qG, 16)
        qg_wp = qG.v

        qstage4 = qstage[:, :].rearrange("p (s e q) -> p s e q", s=4, e=C)
        qS_qh = {}
        dv_qev_last = {}
        pe_q_last = {}
        qS_b1q = {}
        EW("q", qS, qS_xlh[NT - 1])
        EW("q", pe, pe_v_last[NT - 1])
        for t in range(2 if QPP else 1):
            qbuf = (B1 if t % 2 == 0 else b1q) if QPP else B1
            EI("q", lambda q, t=t, qbuf=qbuf: q.dma_start(sb3(qbuf), xt3(xlh, t * TT)), qS, 16)
            qS_b1q[t] = qS.v
        for t in range(NT):
            qbuf = (B1 if t % 2 == 0 else b1q) if QPP else B1
            if not QPP and t >= 1:
                EW("q", qS, qS_xlh[t])
                EW("q", pe, pe_q_last[t - 1])
                EI("q", lambda q, t=t: q.dma_start(sb3(B1), xt3(xlh, t * TT)), qS, 16)
                qS_b1q[t] = qS.v
            EW("t", qS, qS_b1q[t])
            if t == 0:
                EW("t", qG, qg_wq)
                EW("t", dv, dv_xln_a1[NT - 1])
            for e in range(C):
                key = f"pb{2 + e % 2}"
                bwait("t", key)
                for c in range(C - 1):
                    E("t", lambda t_, e=e, c=c, qbuf=qbuf: t_.matmul(
                        pb[2 + e % 2][:, :],
                        wk_a1[:, c * D + e * 128: c * D + (e + 1) * 128],
                        qbuf[:, c * TT:(c + 1) * TT],
                        start=(c == 0), stop=False))
                pe_q = EI("t", lambda t_, e=e, qbuf=qbuf: t_.matmul(
                    pb[2 + e % 2][:, :],
                    wk_a1[:, (C - 1) * D + e * 128: (C - 1) * D + (e + 1) * 128],
                    qbuf[:, (C - 1) * TT:C * TT],
                    start=False, stop=True), pe)
                guards[key] = (pe, pe_q)
                EW("v", pe, pe_q)
                if t > 0 and e == 0:
                    EW("v", qS, qS_qh[(t - 1, 3)])
                EI("v", lambda v, e=e: v.tensor_scalar(
                    qstage4[:, :, e, :], pb[2 + e % 2][:, :],
                    bias_cols[:, 32 + e:33 + e], None, AO.add), dv)
                bset(key, dv)
            pe_q_last[t] = pe.v
            dv_qev_last[t] = dv.v
            for sub in range(4):
                EW("q", dv, dv_qev_last[t])
                EI("q", lambda q, t=t, sub=sub: q.dma_start(
                    qh[t * 4 + sub], qstage4[:, sub]), qS, 16)
                qS_qh[(t, sub)] = qS.v
            if QPP and t + 2 < NT:
                qbuf2 = B1 if t % 2 == 0 else b1q
                EW("q", qS, qS_xlh[t + 2])
                EW("q", pe, pe_q_last[t])
                EI("q", lambda q, t=t, qbuf2=qbuf2: q.dma_start(
                    sb3(qbuf2), xt3(xlh, (t + 2) * TT)), qS, 16)
                qS_b1q[t + 2] = qS.v


    if stage == 2:
        EW("g", dv, dv_qev_last[NT - 1])
        EW("g", qS, qS.v)
        EI("g", lambda g: g.dma_start(y.rearrange("(a p) t -> p a t", p=128), bigA[:, :].rearrange("p (a t) -> p a t", a=8)), qG, 16)
        EW("g", qG, qG.v)
        EW("q", qS, qS.v)

    if stage >= 3:
        barrier()

        bp_row = carve("bp_row_a2", [1, D], BF16, 51200)
        EI("g", lambda g: g.dma_start(bp_row[0:1, :], bproj[None, :]), qG, 16)
        qg_bpr = qG.v

        qgs = [qg0, qg1]
        qS_qg = {}
        EW("q", qS, qS_qh[(NT - 1, 3)])
        for qi in range(2):
            EI("q", lambda q, qi=qi: q.dma_start(
                qgs[qi % 2][:, :], qh[qi].rearrange("p c q -> p (c q)")), qS, 16)
            qS_qg[qi] = qS.v

        dv_z = {}
        ac_sev_half = {}
        ac_exp = {}
        ac_dl = {}
        qS_dl = {}
        pe_av = {}
        pe_pj = {}
        pe_sc_last = {}
        ac_sev_last = {}

        def a2_scores(qt):
            ac_half = [0]
            qg = qgs[qt % 2]
            EW("t", qS, qS_qg[qt])
            for kb in range(8):
                key = f"pb{2 + kb % 4}"
                bwait("t", key)
                for c in range(C - 1):
                    E("t", lambda t_, kb=kb, c=c, qg=qg: t_.matmul(
                        pb[2 + kb % 4][:, :],
                        qg[:, c * 128:(c + 1) * 128],
                        bigA[:, c * S + kb * 512: c * S + (kb + 1) * 512],
                        start=(c == 0), stop=False))
                pe_sc = EI("t", lambda t_, kb=kb, qg=qg: t_.matmul(
                    pb[2 + kb % 4][:, :],
                    qg[:, (C - 1) * 128:C * 128],
                    bigA[:, (C - 1) * S + kb * 512: (C - 1) * S + (kb + 1) * 512],
                    start=False, stop=True), pe)
                guards[key] = (pe, pe_sc)
                EW("a", pe, pe_sc)
                if kb == 0 and qt > 0:
                    EW("a", dv, dv_z[qt - 1])
                EI("a", lambda a, kb=kb: a.activation(
                    s_sb[:, kb * 512:(kb + 1) * 512], pb[2 + kb % 4][:, :], AF.Copy,
                    scale=0.125), ac)
                if kb == 3:
                    ac_half[0] = ac.v
                bset(key, ac)
            pe_sc_last[qt] = pe.v
            ac_sev_last[qt] = ac.v
            ac_sev_half[qt] = ac_half[0]
            if qt + 2 < NQ:
                EW("q", pe, pe_sc_last[qt])
                EI("q", lambda q, qt=qt: q.dma_start(
                    qgs[qt % 2][:, :], qh[qt + 2].rearrange("p c q -> p (c q)")), qS, 16)
                qS_qg[qt + 2] = qS.v

        def a2_chain(qt):
            if qt > 0:
                EW("v", ac, ac_exp[qt - 1])
            for hh in range(2):
                sl = slice(hh * 2048, (hh + 1) * 2048)
                EW("v", ac, ac_sev_half[qt] if hh == 0 else ac_sev_last[qt])
                E("v", lambda v, sl=sl: v.tensor_scalar(work[:, sl], s_sb[:, sl], T4, K27, AO.is_lt, AO.mult))
                E("v", lambda v, sl=sl: v.scalar_tensor_tensor(work[:, sl], s_sb[:, sl], T3, work[:, sl], AO.is_lt, AO.add))
                E("v", lambda v, sl=sl: v.tensor_scalar(work[:, sl], work[:, sl], -1.4, 2.0, AO.mult, AO.add))
                E("v", lambda v, sl=sl: v.scalar_tensor_tensor(work[:, sl], s_sb[:, sl], T2, work[:, sl], AO.is_gt, AO.add))
                E("v", lambda v, sl=sl: v.scalar_tensor_tensor(work[:, sl], s_sb[:, sl], T1, work[:, sl], AO.is_gt, AO.add))
                if hh == 0:
                    E("v", lambda v, sl=sl: v.tensor_tensor(work[:, sl], work[:, sl], s_sb[:, sl], AO.mult))
                else:
                    dv_z[qt] = EI("v", lambda v, sl=sl: v.tensor_tensor(
                        work[:, sl], work[:, sl], s_sb[:, sl], AO.mult), dv)

        def a2_rest(qt):
            q0 = qt * 128
            dv_mx = EI("v", lambda v: v.reduce_max(
                smalls[:, 0:1], work[:, :], axis=AX.X, negate=True), dv)
            EW("v", dv, dv_mx)
            dv_nm = EI("v", lambda v: v.tensor_scalar(
                smalls[:, 1:2], smalls[:, 0:1], 0.5, None, AO.mult), dv)
            EW("a", dv, dv_nm)
            if qt > 0:
                EW("a", pe, pe_av[qt - 1])
            ac_exp[qt] = EI("a", lambda a: a.activation(
                p_sb[:, :], work[:, :], AF.Exp, bias=smalls[:, 1:2], scale=0.5,
                accum_out=smalls[:, 2:3]), ac)
            EW("v", ac, ac_exp[qt])
            dv_rl = EI("v", lambda v: v.reciprocal(smalls[:, 3:4], smalls[:, 2:3]), dv)
            EW("t", ac, ac_exp[qt])
            bwait("t", "pb0")
            bwait("t", "pb1")
            ac_pT = {}

            def t_blk(blk):
                key = f"pt{blk % 2}"
                bwait("t", key)
                for j in range(7):
                    kc = blk * 8 + j
                    E("t", lambda t_, blk=blk, kc=kc: t_.transpose(
                        pt[blk % 2][:, (kc % 8) * 128:(kc % 8 + 1) * 128],
                        p_sb[:, kc * 128:(kc + 1) * 128], ident[:, :]))
                kc = blk * 8 + 7
                pe_T = EI("t", lambda t_, blk=blk, kc=kc: t_.transpose(
                    pt[blk % 2][:, (kc % 8) * 128:(kc % 8 + 1) * 128],
                    p_sb[:, kc * 128:(kc + 1) * 128], ident[:, :]), pe)
                guards[key] = (pe, pe_T)
                EW("a", pe, pe_T)
                ac_pT[blk] = EI("a", lambda a, blk=blk: a.activation(
                    p_sb[:, blk * 1024:(blk + 1) * 1024], pt[blk % 2][:, :], AF.Copy), ac)
                bset(key, ac)

            def av_blk(blk, last):
                EW("t", ac, ac_pT[blk])
                for j in range(8):
                    kc = blk * 8 + j
                    for h in range(2):
                        if last and kc == 31 and h == 1:
                            continue
                        E("t", lambda t_, kc=kc, h=h: t_.matmul(
                            pb[h][:, :],
                            p_sb[:, kc * 128:(kc + 1) * 128],
                            bigB[:, kc * D + h * 512: kc * D + (h + 1) * 512],
                            start=(kc == 0), stop=(kc == 31)))

            t_blk(0)
            t_blk(1)
            av_blk(0, False)
            t_blk(2)
            av_blk(1, False)
            t_blk(3)
            av_blk(2, False)
            av_blk(3, True)
            pe_av[qt] = EI("t", lambda t_: t_.matmul(
                pb[1][:, :], p_sb[:, 31 * 128:32 * 128],
                bigB[:, 31 * D + 512: 32 * D],
                start=False, stop=True), pe)
            guards["pb0"] = (pe, pe_av[qt])
            guards["pb1"] = (pe, pe_av[qt])
            EW("a", pe, pe_av[qt])
            EW("a", dv, dv_rl)
            if qt > 0:
                EW("a", pe, pe_pj[qt - 1])
            E("a", lambda a: a.activation(
                aout[:, 0:512], pb[0][:, :], AF.Copy, scale=smalls[:, 3:4]))
            ac_ao = EI("a", lambda a: a.activation(
                aout[:, 512:1024], pb[1][:, :], AF.Copy, scale=smalls[:, 3:4]), ac)
            bset("pb0", ac)
            bset("pb1", ac)
            EW("t", ac, ac_ao)
            bwait("t", "pt0")
            for j in range(7):
                E("t", lambda t_, j=j: t_.transpose(
                    pt[0][:, j * 128:(j + 1) * 128], aout[:, j * 128:(j + 1) * 128], ident[:, :]))
            pe_aT = EI("t", lambda t_: t_.transpose(
                pt[0][:, 7 * 128:8 * 128], aout[:, 7 * 128:8 * 128], ident[:, :]), pe)
            guards["pt0"] = (pe, pe_aT)
            EW("a", pe, pe_aT)
            ac_aT = EI("a", lambda a: a.activation(aout[:, :], pt[0][:, :], AF.Copy), ac)
            bset("pt0", ac)
            EW("t", ac, ac_aT)
            if qt == 0:
                EW("t", qG, qg_wp)
                EW("t", qG, qg_bpr)
            bwait("t", "pb0")
            bwait("t", "pb1")
            for e in range(C):
                E("t", lambda t_, e=e: t_.matmul(
                    pb[e // 4][:, (e % 4) * 128:(e % 4 + 1) * 128],
                    bp_row[0:1, e * 128:(e + 1) * 128],
                    ones_row[0:1, 0:128],
                    start=True, stop=False))
                for c in range(C):
                    if e == C - 1 and c == C - 1:
                        continue
                    E("t", lambda t_, e=e, c=c: t_.matmul(
                        pb[e // 4][:, (e % 4) * 128:(e % 4 + 1) * 128],
                        wqp[:, c * D + e * 128: c * D + (e + 1) * 128],
                        aout[:, c * 128:(c + 1) * 128],
                        start=False, stop=(c == C - 1)))
            pe_pj[qt] = EI("t", lambda t_: t_.matmul(
                pb[1][:, 3 * 128:4 * 128],
                wqp[:, (C - 1) * D + (C - 1) * 128: (C - 1) * D + C * 128],
                aout[:, (C - 1) * 128:C * 128],
                start=False, stop=True), pe)
            guards["pb0"] = (pe, pe_pj[qt])
            guards["pb1"] = (pe, pe_pj[qt])
            EW("a", pe, pe_pj[qt])
            if qt > 0:
                EW("a", qS, qS_dl[qt - 1])
            E("a", lambda a: a.activation(delta[:, 0:512], pb[0][:, :], AF.Copy))
            ac_dl[qt] = EI("a", lambda a: a.activation(delta[:, 512:1024], pb[1][:, :], AF.Copy), ac)
            bset("pb0", ac)
            bset("pb1", ac)
            EW("q", ac, ac_dl[qt])
            dl_dst = dlh if stage >= 4 else y
            EI("q", lambda q, q0=q0, dl_dst=dl_dst: q.dma_start(
                dl_dst.rearrange("(c p) t -> p c t", p=128)[:, :, q0:q0 + 128],
                delta[:, :].rearrange("p (c q) -> p c q", c=C)), qS, 16)
            qS_dl[qt] = qS.v

        a2_scores(0)
        for qt in range(NQ):
            a2_chain(qt)
            if qt + 1 < NQ:
                a2_scores(qt + 1)
            a2_rest(qt)

        qg_preW = qG.v
        EW("g", pe, pe_sc_last[NQ - 1])
        for c in range(C):
            EI("g", lambda g, c=c: g.dma_start(
                bigA[:, c * WF:(c + 1) * WF], w1[c * 128:(c + 1) * 128, :]), qG, 16)
        qg_w1_2 = qG.v
        EW("g", pe, pe_av[NQ - 1])
        for w in range(WC):
            EI("g", lambda g, w=w: g.dma_start(
                bigB[:, w * D:(w + 1) * D], w2[w * 128:(w + 1) * 128, :]), qG, 16)
        qg_w2_2 = qG.v

    if stage >= 4:
        barrier(qg_val=qg_preW)

        acc_gate = {}
        for t0_ in range(2):
            tgt = xa if t0_ % 2 == 0 else xa2
            EI("q", lambda q, t0_=t0_, tgt=tgt: q.dma_start(
                sb3(tgt), xt3(x1h, t0_ * TT)), qS, 16)
            qS_load[t0_] = qS.v
            EW("g", qS, qS_load[t0_])
            EI("g", lambda g, t0_=t0_, tgt=tgt: g.dma_start(
                sb3(tgt), xt3(dlh, t0_ * TT), accum_op=AO.add), qG, 16)
            acc_gate[t0_] = qG.v
        emit_ffn(True, qg_w1_2, qg_w2_2, acc_gate)

    EW("q", qS, qS.v)

    with nc.Block() as block:
        @block.tensor
        def _(eng):
            for fn in ops["t"]:
                fn(eng)

        @block.scalar
        def _(eng):
            for fn in ops["a"]:
                fn(eng)

        @block.vector
        def _(eng):
            for fn in ops["v"]:
                fn(eng)

        @block.gpsimd
        def _(eng):
            for fn in ops["g"]:
                fn(eng)

        @block.sync
        def _(eng):
            for fn in ops["q"]:
                fn(eng)

    return nc


_NC_CACHE = None


def kernel(L0_attn_x, L0_attn_W_qkv, L0_attn_b_qkv, L0_attn_W_proj, L0_attn_b_proj,
           L0_ffn_w1, L0_ffn_b1, L0_ffn_w2, L0_ffn_b2):
    global _NC_CACHE
    xs = np.asarray(L0_attn_x, dtype=np.float32)
    B = xs.shape[0]
    if _NC_CACHE is None:
        _NC_CACHE = build_nc()
    nc = _NC_CACHE
    common = {
        "wqkv": np.ascontiguousarray(L0_attn_W_qkv, dtype=np.float32),
        "bqkv": np.ascontiguousarray(L0_attn_b_qkv, dtype=np.float32),
        "wproj": np.ascontiguousarray(L0_attn_W_proj, dtype=np.float32),
        "bproj": np.ascontiguousarray(L0_attn_b_proj, dtype=np.float32),
        "w1": np.ascontiguousarray(L0_ffn_w1, dtype=np.float32),
        "b1": np.ascontiguousarray(L0_ffn_b1, dtype=np.float32),
        "w2": np.ascontiguousarray(L0_ffn_w2, dtype=np.float32),
        "b2": np.ascontiguousarray(L0_ffn_b2, dtype=np.float32),
    }
    in_maps = []
    for c in range(B):
        m = dict(common)
        m["x"] = np.ascontiguousarray(xs[c].T)
        in_maps.append(m)
    res = run_bass_kernel_spmd(nc, in_maps, core_ids=list(range(B)))
    out = np.stack([np.asarray(r["y"]).T for r in res.results], axis=0)
    return out.astype(np.float32)
